# version 01 (final)
# baseline (speedup 1.0000x reference)
import contextlib

import numpy as np
import ml_dtypes
import concourse.bass as bass
import concourse.mybir as mybir
from concourse.bass_utils import run_bass_kernel_spmd

F32 = mybir.dt.float32
BF16 = mybir.dt.bfloat16
I32 = mybir.dt.int32
AF = mybir.ActivationFunctionType
ALU = mybir.AluOpType
BF = ml_dtypes.bfloat16

HID = 4096
E_TOT = 18560
INTER = 8192
CONVD = 10240
H = 128
P = 64
G = 8
CH = 128
T = 1024
NZ = 8
ET_GATE = list(range(0, 64))
ET_HBC = list(range(64, 144))
ET_DT = 144
EPS = 1e-5

DEBUG_TAPS = ()
PHASES = "p4"
P4_CHUNKS = 8
P4_LEVEL = "full"


def _u(name, cnt=[0]):
    cnt[0] += 1
    return f"{name}_{cnt[0]}"


class Bld:
    def __init__(self, nc, block, stack):
        self.nc = nc
        self.block = block
        self.stack = stack
        self.sems = {}
        self.cnt = {}

    def sem(self, *names):
        for n in names:
            self.sems[n] = self.stack.enter_context(self.nc.semaphore(n))
            self.cnt[n] = 0

    def bump(self, name, amt):
        self.cnt[name] += amt
        return self.cnt[name]

    def barrier(self, engines=("sync", "tensor", "vector", "scalar", "gpsimd")):
        snap = [(n, v) for n, v in self.cnt.items() if v > 0]
        for e in engines:
            dec = getattr(self.block, e)

            def _f(eng, snap=snap):
                for n, v in snap:
                    eng.wait_ge(self.sems[n], v)
            dec(_f)



def emit_L1(nc, bl, io, dbg):
    blk = bl.block
    S = bl.sems
    C = bl.cnt
    sb = lambda name, shape, dt: nc.sbuf_tensor(_u(name), shape, dt)
    ps = lambda name, shape, dt: nc.psum_tensor(_u(name), shape, dt)

    gst = contextlib.ExitStack()
    BC = gst.enter_context(sb("BC", [128, 16 * 1024], BF16))
    CONVW = gst.enter_context(sb("CONVW", [128, 320], F32))
    CONVB = gst.enter_context(sb("CONVB", [128, 80], F32))
    COLS = gst.enter_context(sb("COLS", [128, 8], F32))
    ID64 = gst.enter_context(sb("ID64", [128, 64], BF16))
    IDBF = gst.enter_context(sb("IDBF", [128, 128], BF16))
    IDF = gst.enter_context(sb("IDF", [128, 128], F32))
    MASK = gst.enter_context(sb("MASK", [128, 128], F32))
    ONESB = gst.enter_context(sb("ONESB", [1, 128], BF16))
    DROW = gst.enter_context(sb("DROW", [1, 128], BF16))
    DBC = gst.enter_context(sb("DBC", [128, 128], F32))
    DTS = gst.enter_context(sb("DTS", [128, 5 * 1024], F32))
    acum_tm = DTS[:, 0 * T:1 * T]
    scA_tm = DTS[:, 1 * T:2 * T]
    dtcol_tm = DTS[:, 2 * T:3 * T]
    dtdec_tm = DTS[:, 3 * T:4 * T]
    decayZbc = DTS[:, 4 * T:5 * T]
    SMALL = gst.enter_context(sb("SMALL", [128, 64], F32))
    USTR = gst.enter_context(sb("USTR", [8, 8], F32))
    SNAP = gst.enter_context(sb("SNAP", [128, 1024], F32)) if "stp" in dbg else None
    dtsb_stack = contextlib.ExitStack()
    DTSB = dtsb_stack.enter_context(sb("DTSB", [128, 6 * 1024], F32))
    dtraw = DTSB[:, 0 * T:1 * T]
    dtv = DTSB[:, 1 * T:2 * T]
    adt = DTSB[:, 2 * T:3 * T]
    acum = DTSB[:, 3 * T:4 * T]
    scr = DTSB[:, 4 * T:5 * T]
    tmp2 = DTSB[:, 5 * T:6 * T]

    bl.sem("dg", "ds", "pe", "dv", "ac", "pf", "mm", "xw", "gw", "hs", "gs", "peW")

    @blk.sync
    def _(sy):
        for dst, src in [(CONVW[:], io["conv_wr"]), (CONVB[:], io["conv_br"]),
                         (COLS[:, 0:1], io["dt_bias_c"]), (COLS[:, 1:2], io["A_log_c"]),
                         (COLS[:, 2:3], io["D_c"]), (ID64[:], io["ident64"]),
                         (IDBF[:], io["identbf"]), (IDF[:], io["identf"]),
                         (MASK[:], io["maskadd"]), (ONESB[:], io["ones_bf"]),
                         (DROW[:], io["D_row"]), (USTR[:], io["ustrict"])]:
            sy.dma_start(out=dst, in_=src).then_inc(S["ds"], 16)
    bl.bump("ds", 16 * 12)
    consts_done = C["ds"]

    @blk.gpsimd
    def _(g):
        g.dma_start(out=io["xbf"], in_=io["x_comb"]).then_inc(S["dg"], 16)
    bl.bump("dg", 16)
    xbf_done = C["dg"]

    with ps("dbc", [128, 128], F32) as dbc_ps:
        @blk.tensor
        def _(te):
            te.wait_ge(S["ds"], consts_done)
            te.matmul(dbc_ps[:], ONESB[0:1, :], DROW[0:1, :], start=True, stop=True).then_inc(S["pe"], 1)
        bl.bump("pe", 1)
        t1 = C["pe"]

        @blk.scalar
        def _(sc):
            sc.wait_ge(S["ds"], consts_done)
            sc.activation(out=COLS[:, 3:4], in_=COLS[:, 1:2], func=AF.Exp).then_inc(S["ac"], 1)
        bl.bump("ac", 1)
        ac0 = C["ac"]

        @blk.vector
        def _(v):
            v.wait_ge(S["pe"], t1)
            v.wait_ge(S["ac"], ac0)
            v.tensor_copy(DBC[:], dbc_ps[:])
            v.memset(COLS[:, 5:6], 1.0)
            v.tensor_scalar_mul(COLS[:, 4:5], COLS[:, 3:4], -1.0).then_inc(S["dv"], 1)
        bl.bump("dv", 1)

    tiles_order = ET_HBC + [ET_DT] + ET_GATE
    supertiles = []
    i = 0
    while i < len(tiles_order):
        if i + 1 < len(tiles_order) and tiles_order[i + 1] == tiles_order[i] + 1 and tiles_order[i] != ET_DT:
            supertiles.append((tiles_order[i], tiles_order[i + 1]))
            i += 2
        else:
            supertiles.append((tiles_order[i],))
            i += 1

    with contextlib.ExitStack() as p1s:
        XT = p1s.enter_context(sb("XT", [128, 32 * 1040], BF16))

        @blk.sync
        def _(sy):
            sy.wait_ge(S["dg"], xbf_done)
            for d in range(32):
                sy.dma_start(out=XT[:, d * 1040:(d + 1) * 1040],
                             in_=io["xbf"][:, d * 128:(d + 1) * 128], transpose=True).then_inc(S["ds"], 16)
        bl.bump("ds", 16 * 32)
        xt_done = C["ds"]
        WST = [p1s.enter_context(sb("WST", [128, 32 * 256], BF16)) for _ in range(2)]
        ACC = [p1s.enter_context(sb("ACC", [128, 1024], F32)) for _ in range(2)]
        HST = [p1s.enter_context(sb("HST", [128, 1024], BF16)) for _ in range(2)]
        PSU = [p1s.enter_context(ps("PJ", [128, 1040], F32)) for _ in range(2)]

        pf_hist = [0, 0]
        hs_hist = [0, 0]
        gs_hist = [0, 0]
        ti = 0
        for st, tiles in enumerate(supertiles):
            rows = 128 * len(tiles)
            r0 = tiles[0] * 128
            pp = st % 2
            w_xw = 512 * (st - 1) if st >= 2 else 0

            @blk.gpsimd
            def _(g, pp=pp, r0=r0, rows=rows, w_xw=w_xw):
                if w_xw:
                    g.wait_ge(S["xw"], w_xw)
                g.dma_start(out=io["wbf"][pp][0:rows, :],
                            in_=io["W_in"][r0:r0 + rows, :]).then_inc(S["gw"], 16)
            bl.bump("gw", 16)
            gw_tgt = C["gw"]
            w_peW = st - 1 if st >= 2 else 0

            @blk.sync
            def _(sy, pp=pp, rows=rows, gw_tgt=gw_tgt, w_peW=w_peW):
                sy.wait_ge(S["gw"], gw_tgt)
                if w_peW:
                    sy.wait_ge(S["peW"], w_peW)
                for d in range(32):
                    sy.dma_start(out=WST[pp][:, d * 256:d * 256 + rows],
                                 in_=io["wbf"][pp][0:rows, d * 128:(d + 1) * 128],
                                 transpose=True).then_inc(S["xw"], 16)
            bl.bump("xw", 16 * 32)
            xw_tgt = C["xw"]

            for li, et in enumerate(tiles):
                pq = ti % 2
                is_gate = et in ET_GATE
                is_dt = et == ET_DT
                w_pf = pf_hist[pq]

                @blk.tensor
                def _(te, pq=pq, pp=pp, li=li, xw_tgt=xw_tgt, w_pf=w_pf, is_gate=is_gate,
                      last=(li == len(tiles) - 1)):
                    te.wait_ge(S["ds"], xt_done)
                    te.wait_ge(S["xw"], xw_tgt)
                    if w_pf:
                        te.wait_ge(S["pf"], w_pf)
                    for d in range(32):
                        lhsT = WST[pp][:, d * 256 + li * 128: d * 256 + (li + 1) * 128]
                        te.ldweights(lhsT)
                        xt = XT[:, d * 1040:(d + 1) * 1040]
                        if is_gate:
                            te.matmul(PSU[pq][:, 0:512], lhsT, xt[:, 16:528], start=(d == 0), stop=(d == 31))
                            i2 = te.matmul(PSU[pq][:, 512:1024], lhsT, xt[:, 528:1040], start=(d == 0), stop=(d == 31))
                        else:
                            te.matmul(PSU[pq][:, 0:512], lhsT, xt[:, 0:512], start=(d == 0), stop=(d == 31))
                            te.matmul(PSU[pq][:, 512:1024], lhsT, xt[:, 512:1024], start=(d == 0), stop=(d == 31))
                            i2 = te.matmul(PSU[pq][:, 1024:1040], lhsT, xt[:, 1024:1040], start=(d == 0), stop=(d == 31))
                    i2.then_inc(S["mm"], 1)
                    if last:
                        te.nop().then_inc(S["peW"], 1)
                bl.bump("mm", 1)
                if li == len(tiles) - 1:
                    bl.bump("peW", 1)
                mm_tgt = C["mm"]

                if (not is_gate) and (not is_dt):
                    er = et - 64
                    w_hs = hs_hist[pq]

                    @blk.vector
                    def _(v, pq=pq, er=er, mm_tgt=mm_tgt, w_hs=w_hs):
                        v.wait_ge(S["mm"], mm_tgt)
                        if w_hs:
                            v.wait_ge(S["hs"], w_hs)
                        a = ACC[pq][:]
                        p = PSU[pq]
                        v.tensor_scalar(a, p[:, 16:1040], CONVW[:, er * 4 + 3:er * 4 + 4], None, ALU.mult)
                        for k in range(3):
                            st_i = v.scalar_tensor_tensor(a, p[:, 13 + k:1037 + k],
                                                          CONVW[:, er * 4 + k:er * 4 + k + 1], a,
                                                          ALU.mult, ALU.add)
                        st_i.then_inc(S["pf"], 1)
                    bl.bump("pf", 1)
                    pf_hist[pq] = C["pf"]
                    pf_now = C["pf"]
                    is_bc = er >= 64
                    w_gs = gs_hist[pq]

                    @blk.scalar
                    def _(sc, pq=pq, er=er, pf_now=pf_now, is_bc=is_bc, w_gs=w_gs):
                        sc.wait_ge(S["pf"], pf_now)
                        if (not is_bc) and w_gs:
                            sc.wait_ge(S["gs"], w_gs)
                        if is_bc:
                            outap = BC[:, (er - 64) * 1024:(er - 63) * 1024]
                        else:
                            outap = HST[pq][:]
                        sc.activation(out=outap, in_=ACC[pq][:], func=AF.Silu,
                                      bias=CONVB[:, er:er + 1]).then_inc(S["hs"], 1)
                    bl.bump("hs", 1)
                    hs_hist[pq] = C["hs"]
                    hs_tgt = C["hs"]

                    if not is_bc:
                        @blk.gpsimd
                        def _(g, pq=pq, er=er, hs_tgt=hs_tgt):
                            g.wait_ge(S["hs"], hs_tgt)
                            g.dma_start(out=io["hs_sp"][er], in_=HST[pq][:]).then_inc(S["gs"], 16)
                        bl.bump("gs", 16)
                        gs_hist[pq] = C["gs"]
                    elif er >= 72:
                        @blk.gpsimd
                        def _(g, er=er, hs_tgt=hs_tgt):
                            g.wait_ge(S["hs"], hs_tgt)
                            g.dma_start(out=io["C_sp"][er - 72],
                                        in_=BC[:, (er - 64) * 1024:(er - 63) * 1024]).then_inc(S["gs"], 16)
                        bl.bump("gs", 16)
                elif is_dt:
                    @blk.vector
                    def _(v, pq=pq, mm_tgt=mm_tgt):
                        v.wait_ge(S["mm"], mm_tgt)
                        v.tensor_copy(dtraw, PSU[pq][:, 16:1040]).then_inc(S["pf"], 1)
                    bl.bump("pf", 1)
                    pf_hist[pq] = C["pf"]
                else:
                    w_gs = gs_hist[pq]

                    @blk.scalar
                    def _(sc, pq=pq, mm_tgt=mm_tgt, w_gs=w_gs):
                        sc.wait_ge(S["mm"], mm_tgt)
                        if w_gs:
                            sc.wait_ge(S["gs"], w_gs)
                        sc.activation(out=HST[pq][:], in_=PSU[pq][:, 0:1024],
                                      func=AF.Silu).then_inc(S["hs"], 1)
                        sc.nop().then_inc(S["pf"], 1)
                    bl.bump("hs", 1)
                    bl.bump("pf", 1)
                    pf_hist[pq] = C["pf"]
                    hs_tgt = C["hs"]

                    @blk.gpsimd
                    def _(g, pq=pq, et=et, hs_tgt=hs_tgt):
                        g.wait_ge(S["hs"], hs_tgt)
                        g.dma_start(out=io["gate_sp"][et], in_=HST[pq][:]).then_inc(S["gs"], 16)
                    bl.bump("gs", 16)
                    gs_hist[pq] = C["gs"]
                ti += 1

        bl.barrier()

    if PHASES == "p1":
        dtsb_stack.close()
        gst.close()
        return

    pf_dt = C["pf"]

    @blk.scalar
    def _(sc):
        sc.activation(out=scr, in_=dtraw, func=AF.Abs, bias=COLS[:, 0:1])
        sc.activation(out=tmp2, in_=scr, func=AF.Exp, scale=-1.0)
        sc.activation(out=scr, in_=tmp2, func=AF.Ln, bias=COLS[:, 5:6]).then_inc(S["ac"], 1)
    bl.bump("ac", 1)
    ac1 = C["ac"]

    @blk.vector
    def _(v):
        v.wait_ge(S["ac"], ac1)
        v.tensor_scalar(dtv, dtraw, COLS[:, 0:1], 0.0, ALU.add, ALU.max)
        v.tensor_tensor(dtv, dtv, scr, ALU.add)
        v.tensor_scalar_min(dtv, dtv, 100.0)
        v.tensor_scalar(adt, dtv, COLS[:, 4:5], None, ALU.mult)
        last = None
        for z in range(NZ):
            c = slice(z * CH, (z + 1) * CH)
            last = v.tensor_tensor_scan(acum[:, c], adt[:, c], adt[:, c], 0.0, ALU.add, ALU.bypass)
        last.then_inc(S["dv"], 1)
        v.wait_ge(S["dv"], C["dv"] + 1)
        azv = acum.rearrange("p (z l) -> p z l", z=NZ)[:, :, 127]
        v.tensor_copy(SMALL[:, 0:8], azv)
        last = None
        for z in range(NZ):
            c = slice(z * CH, (z + 1) * CH)
            last = v.tensor_scalar(scr[:, c], acum[:, c], acum[:, z * CH + 127:z * CH + 128], None, ALU.subtract)
        last.then_inc(S["dv"], 1)
    bl.bump("dv", 2)
    dv1 = C["dv"]

    @blk.scalar
    def _(sc):
        sc.wait_ge(S["dv"], dv1)
        sc.activation(out=tmp2, in_=scr, func=AF.Exp, scale=-1.0).then_inc(S["ac"], 1)
    bl.bump("ac", 1)
    ac2 = C["ac"]

    @blk.vector
    def _(v):
        v.wait_ge(S["ac"], ac2)
        v.tensor_tensor(scr, dtv, tmp2, ALU.mult).then_inc(S["dv"], 1)
    bl.bump("dv", 1)

    with contextlib.ExitStack() as p3s:
        tp = [p3s.enter_context(ps("dtp", [128, 512], F32)) for _ in range(2)]
        tpb = [p3s.enter_context(ps("dtpb", [128, 512], F32)) for _ in range(2)]
        tpc = [p3s.enter_context(ps("dtpc", [128, 512], F32)) for _ in range(2)]
        gz_sb = p3s.enter_context(sb("gzsb", [8, 128], F32))
        az_sb = p3s.enter_context(sb("azsb", [8, 128], F32))
        ev_hist = [0, 0]
        for z in range(NZ):
            c = slice(z * CH, (z + 1) * CH)
            pq = z % 2
            w_dv = C["dv"]
            w_ev = ev_hist[pq]

            @blk.tensor
            def _(te, c=c, pq=pq, w_dv=w_dv, w_ev=w_ev):
                te.wait_ge(S["dv"], w_dv)
                if w_ev:
                    te.wait_ge(S["dv"], w_ev)
                te.transpose(tp[pq][:, 0:128], acum[:, c], IDF[:])
                te.transpose(tpb[pq][:, 0:128], dtv[:, c], IDF[:])
                te.transpose(tpc[pq][:, 0:128], scr[:, c], IDF[:]).then_inc(S["pe"], 1)
            bl.bump("pe", 1)
            pe_tgt = C["pe"]

            @blk.vector
            def _(v, c=c, pq=pq, pe_tgt=pe_tgt):
                v.wait_ge(S["pe"], pe_tgt)
                v.tensor_copy(acum_tm[:, c], tp[pq][:, 0:128])
                v.tensor_copy(dtcol_tm[:, c], tpb[pq][:, 0:128])
                v.tensor_copy(dtdec_tm[:, c], tpc[pq][:, 0:128]).then_inc(S["dv"], 1)
            bl.bump("dv", 1)
            ev_hist[pq] = C["dv"]

        if "stp" in dbg:
            w_snap = C["dv"]

            @blk.vector
            def _(v):
                v.wait_ge(S["dv"], w_snap)
                v.tensor_copy(SNAP[:], dtdec_tm).then_inc(S["dv"], 1)
            bl.bump("dv", 1)
        w_dv = C["dv"]

        @blk.tensor
        def _(te, w_dv=w_dv):
            te.wait_ge(S["dv"], w_dv)
            te.transpose(tp[0][0:8, 0:128], SMALL[:, 0:8], IDF[:]).then_inc(S["pe"], 1)
        bl.bump("pe", 1)
        pe2 = C["pe"]

        @blk.vector
        def _(v):
            v.wait_ge(S["pe"], pe2)
            v.tensor_copy(az_sb[:], tp[0][0:8, 0:128]).then_inc(S["dv"], 1)
        bl.bump("dv", 1)
        dv4a = C["dv"]

        @blk.tensor
        def _(te):
            te.wait_ge(S["dv"], dv4a)
            te.matmul(tp[1][0:8, 0:128], USTR[0:8, 0:8], az_sb[:], start=True, stop=True).then_inc(S["pe"], 1)
        bl.bump("pe", 1)
        pe3 = C["pe"]

        @blk.scalar
        def _(sc):
            sc.wait_ge(S["pe"], pe3)
            sc.activation(out=gz_sb[:], in_=tp[1][0:8, 0:128], func=AF.Exp).then_inc(S["ac"], 1)
        bl.bump("ac", 1)

        @blk.vector
        def _(v):
            v.nop().then_inc(S["dv"], 1)
        bl.bump("dv", 1)
        dv4 = C["dv"]

        @blk.scalar
        def _(sc):
            sc.wait_ge(S["dv"], dv4)
            sc.activation(out=scA_tm, in_=acum_tm, func=AF.Exp).then_inc(S["ac"], 1)
        bl.bump("ac", 1)
        ac3 = C["ac"]

        @blk.gpsimd
        def _(g):
            g.wait_ge(S["dv"], dv4)
            g.wait_ge(S["ac"], ac3)
            g.dma_start(out=io["acum_d"].transpose([1, 0, 2]), in_=acum.rearrange("p (z l) -> p z l", z=8)).then_inc(S["dg"], 16)
            g.dma_start(out=io["az_d"], in_=az_sb[:]).then_inc(S["dg"], 16)
            g.dma_start(out=io["gamzh"], in_=gz_sb[:]).then_inc(S["dg"], 16)
            g.dma_start(out=io["expac"], in_=scA_tm).then_inc(S["dg"], 16)
        bl.bump("dg", 64)
        dg_tgt = C["dg"]

        @blk.sync
        def _(sy):
            sy.wait_ge(S["dg"], dg_tgt)
            src = io["az_d"].rearrange("a b -> (a b)").unsqueeze(0).partition_broadcast(128)
            sy.dma_start(out=tmp2, in_=src).then_inc(S["ds"], 16)
        bl.bump("ds", 16)
        ds_tgt = C["ds"]

        @blk.scalar
        def _(sc):
            sc.wait_ge(S["ds"], ds_tgt)
            sc.activation(out=decayZbc, in_=tmp2, func=AF.Exp).then_inc(S["ac"], 1)
        bl.bump("ac", 1)

    bl.barrier()
    if "stp" in dbg:
        @blk.gpsimd
        def _(g):
            g.dma_start(out=io["dbg_dtdec_pre"], in_=dtdec_tm).then_inc(S["dg"], 16)
            g.dma_start(out=io["dbg_scr"], in_=scr).then_inc(S["dg"], 16)
            g.dma_start(out=io["dbg_tmp2"], in_=tmp2).then_inc(S["dg"], 16)
            g.dma_start(out=io["dbg_dtvfm"], in_=dtv).then_inc(S["dg"], 16)
            g.dma_start(out=io["dbg_snap"], in_=SNAP[:]).then_inc(S["dg"], 16)
            g.wait_ge(S["dg"], C["dg"] + 80)
        bl.bump("dg", 80)
    if "p3" in dbg:
        @blk.gpsimd
        def _(g):
            g.dma_start(out=io["dbg_dtraw"], in_=dtraw).then_inc(S["dg"], 16)
            g.dma_start(out=io["dbg_dtv"], in_=dtv).then_inc(S["dg"], 16)
            g.dma_start(out=io["dbg_acum"], in_=acum).then_inc(S["dg"], 16)
            g.dma_start(out=io["dbg_acumtm"], in_=acum_tm).then_inc(S["dg"], 16)
            g.dma_start(out=io["dbg_cols"], in_=COLS[:]).then_inc(S["dg"], 16)
            g.dma_start(out=io["dbg_adt"], in_=adt).then_inc(S["dg"], 16)
            g.dma_start(out=io["dbg_small"], in_=SMALL[:]).then_inc(S["dg"], 16)
            g.wait_ge(S["dg"], C["dg"] + 112)
        bl.bump("dg", 112)
    dtsb_stack.close()
    if PHASES == "p3":
        gst.close()
        return

    with contextlib.ExitStack() as st4:
        HSC = [st4.enter_context(sb("HSC", [128, 8192], BF16)) for _ in range(2)]
        Y = st4.enter_context(sb("Y", [128, 8192], BF16))
        CARF = st4.enter_context(sb("CARF", [128, 8192], F32))
        CARB = st4.enter_context(sb("CARB", [128, 8192], BF16))
        RB = [st4.enter_context(sb("RB", [128, 2048], F32)) for _ in range(2)]
        LSC = st4.enter_context(sb("LSC", [128, 2048], F32))
        M2 = [st4.enter_context(sb("M2", [128, 2048], BF16)) for _ in range(2)]
        M1S = st4.enter_context(sb("M1S", [128, 1024], F32))
        BTM = st4.enter_context(sb("BTM", [128, 1024], BF16))
        XCH = [st4.enter_context(sb("XCH", [128, 512], BF16)) for _ in range(2)]
        XCD = [st4.enter_context(sb("XCD", [128, 512], BF16)) for _ in range(2)]
        TMP = [st4.enter_context(sb("TMPQ", [128, 512], F32)) for _ in range(2)]
        TPS = [st4.enter_context(ps("TPS", [128, 512], BF16)) for _ in range(2)]
        YDP = [st4.enter_context(ps("YDP", [128, 512], F32)) for _ in range(2)]
        YOP = [st4.enter_context(ps("YOP", [128, 512], F32)) for _ in range(2)]
        STP = [st4.enter_context(ps("STP", [128, 512], F32)) for _ in range(2)]

        DBGT = st4.enter_context(sb("DBGT", [128, 2048], F32)) if "stpgb" in dbg else None

        @blk.vector
        def _(v):
            v.memset(CARF[:], 0.0)
            v.memset(CARB[:], 0.0).then_inc(S["dv"], 1)
        bl.bump("dv", 1)

        def prefetch(zz, wait_pe):
            @blk.gpsimd
            def _(g, zz=zz, wait_pe=wait_pe):
                if wait_pe:
                    g.wait_ge(S["pe"], wait_pe)
                for half in range(2):
                    g.dma_start(
                        out=HSC[zz % 2][:].rearrange("p (a b) -> p a b", a=64)[:, half * 32:(half + 1) * 32, :],
                        in_=io["hs_sp"][half * 32:(half + 1) * 32, :, zz * 128:(zz + 1) * 128].transpose([1, 0, 2])
                    ).then_inc(S["dg"], 16)
            bl.bump("dg", 32)

        prefetch(0, 0)
        tps_hist = [0, 0]
        psY_hist = [0, 0]
        gbt = 0
        for z in range(P4_CHUNKS):
            hs_ready = C["dg"]
            if z + 1 < P4_CHUNKS:
                prefetch(z + 1, C["pe"])

            for rnd in range(2):
                gs_r = list(range(rnd * 4, rnd * 4 + 4))
                w_dv = C["dv"]

                @blk.tensor
                def _(te, z=z, gs_r=gs_r, w_dv=w_dv):
                    te.wait_ge(S["dv"], w_dv)
                    for j, g_ in enumerate(gs_r):
                        bsl = BC[:, g_ * 1024 + z * 128: g_ * 1024 + (z + 1) * 128]
                        te.transpose(TPS[0][:, j * 128:(j + 1) * 128], bsl, IDBF[:])
                    for j, g_ in enumerate(gs_r):
                        bsl = BC[:, g_ * 1024 + z * 128: g_ * 1024 + (z + 1) * 128]
                        csl = BC[:, (8 + g_) * 1024 + z * 128: (8 + g_) * 1024 + (z + 1) * 128]
                        te.matmul(YDP[0][:, j * 128:(j + 1) * 128], bsl, csl,
                                  start=True, stop=True).then_inc(S["pe"], 1)
                bl.bump("pe", 4)
                pe_tgt = C["pe"]

                @blk.vector
                def _(v, rnd=rnd, pe_tgt=pe_tgt):
                    v.wait_ge(S["pe"], pe_tgt)
                    v.tensor_copy(BTM[:, rnd * 512:(rnd + 1) * 512], TPS[0][:])
                    v.tensor_copy(M1S[:, rnd * 512:(rnd + 1) * 512], YDP[0][:]).then_inc(S["dv"], 1)
                bl.bump("dv", 1)
                tps_hist[0] = C["dv"]
                tps_hist[1] = C["dv"]
                psY_hist[0] = max(psY_hist[0], C["dv"])
                psY_hist[1] = max(psY_hist[1], C["dv"])

            for g_ in range(G):
                if P4_LEVEL == "pre":
                    break
                dv_now = C["dv"]

                @blk.sync
                def _(sy, z=z, g_=g_, dv_now=dv_now):
                    if g_ >= 2 or z > 0:
                        sy.wait_ge(S["dv"], dv_now)
                    src = io["acum_d"][z, g_ * 16:(g_ + 1) * 16, :].rearrange("h l -> (h l)").unsqueeze(0).partition_broadcast(128)
                    sy.dma_start(out=RB[g_ % 2][:], in_=src).then_inc(S["ds"], 16)
                bl.bump("ds", 16)
                r_tgt = C["ds"]

                @blk.vector
                def _(v, z=z, g_=g_, r_tgt=r_tgt):
                    v.wait_ge(S["ds"], r_tgt)
                    for hh in range(16):
                        h = g_ * 16 + hh
                        ii = v.scalar_tensor_tensor(RB[g_ % 2][:, hh * 128:(hh + 1) * 128],
                                                    RB[g_ % 2][:, hh * 128:(hh + 1) * 128],
                                                    acum_tm[:, z * 128 + h:z * 128 + h + 1],
                                                    MASK[:], ALU.subtract, ALU.add)
                    ii.then_inc(S["dv"], 1)
                bl.bump("dv", 1)
                dh_tgt = C["dv"]

                @blk.scalar
                def _(sc, g_=g_, dh_tgt=dh_tgt):
                    sc.wait_ge(S["dv"], dh_tgt)
                    sc.activation(out=LSC[:], in_=RB[g_ % 2][:], func=AF.Exp).then_inc(S["ac"], 1)
                bl.bump("ac", 1)
                l_tgt = C["ac"]

                @blk.vector
                def _(v, g_=g_, l_tgt=l_tgt):
                    v.wait_ge(S["ac"], l_tgt)
                    m1b = M1S[:, g_ * 128:(g_ + 1) * 128].unsqueeze(1).broadcast_to([128, 16, 128])
                    v.tensor_tensor(M2[g_ % 2][:].rearrange("p (a b) -> p a b", a=16),
                                    LSC[:].rearrange("p (a b) -> p a b", a=16),
                                    m1b, ALU.mult).then_inc(S["dv"], 1)
                bl.bump("dv", 1)
                m2_tgt = C["dv"]

                for sub in range(2):
                    if P4_LEVEL == "m2":
                        break
                    gb = g_ * 2 + sub
                    h0 = gb * 8
                    tb = gbt % 2
                    w_tps = tps_hist[tb]

                    @blk.tensor
                    def _(te, z=z, h0=h0, tb=tb, w_tps=w_tps, hs_ready=hs_ready):
                        te.wait_ge(S["dg"], hs_ready)
                        if w_tps:
                            te.wait_ge(S["dv"], w_tps)
                        for jt in range(4):
                            e = h0 // 2 + jt
                            src = HSC[z % 2][:, e * 128:(e + 1) * 128]
                            te.transpose(TPS[tb][:, jt * 128:(jt + 1) * 128], src,
                                         IDBF[:]).then_inc(S["pe"], 1)
                    bl.bump("pe", 4)
                    t_tgt = C["pe"]

                    if P4_LEVEL == "tps0":
                        gbt += 1
                        continue

                    @blk.vector
                    def _(v, z=z, h0=h0, tb=tb, gb=gb, t_tgt=t_tgt):
                        v.wait_ge(S["pe"], t_tgt)
                        dcb = dtcol_tm[:, z * 128 + h0:z * 128 + h0 + 8].unsqueeze(2).broadcast_to([128, 8, 64])
                        ddb = dtdec_tm[:, z * 128 + h0:z * 128 + h0 + 8].unsqueeze(2).broadcast_to([128, 8, 64])
                        Db = DBC[:, h0:h0 + 8].unsqueeze(2).broadcast_to([128, 8, 64])
                        t3 = TPS[tb][:].rearrange("p (a b) -> p a b", a=8)
                        v.tensor_tensor(XCH[tb][:].rearrange("p (a b) -> p a b", a=8), t3, dcb, ALU.mult)
                        v.tensor_tensor(XCD[tb][:].rearrange("p (a b) -> p a b", a=8), t3, ddb, ALU.mult)
                        v.tensor_tensor(Y[:, gb * 512:(gb + 1) * 512].rearrange("p (a b) -> p a b", a=8),
                                        t3, Db, ALU.mult).then_inc(S["dv"], 1)
                    bl.bump("dv", 1)
                    x_tgt = C["dv"]
                    tps_hist[tb] = x_tgt
                    w_asm = psY_hist[tb]
                    if P4_LEVEL == "tps":
                        gbt += 1
                        continue

                    @blk.tensor
                    def _(te, z=z, g_=g_, h0=h0, tb=tb, x_tgt=x_tgt, m2_tgt=m2_tgt, w_asm=w_asm):
                        te.wait_ge(S["dv"], max(x_tgt, m2_tgt))
                        if w_asm:
                            te.wait_ge(S["dv"], w_asm)
                        cfm = BC[:, (8 + g_) * 1024 + z * 128:(8 + g_) * 1024 + (z + 1) * 128]
                        te.ldweights(cfm)
                        for j in range(8):
                            h = h0 + j
                            te.matmul(YOP[tb][:, j * 64:(j + 1) * 64], cfm,
                                      CARB[:, h * 64:(h + 1) * 64], start=True, stop=True)
                        btm = BTM[:, g_ * 128:(g_ + 1) * 128]
                        te.ldweights(btm)
                        for j in range(8):
                            te.matmul(STP[tb][:, j * 64:(j + 1) * 64], btm,
                                      XCD[tb][:, j * 64:(j + 1) * 64], start=True, stop=True)
                        for j in range(8):
                            h = h0 + j
                            hh = h - g_ * 16
                            m2s = M2[g_ % 2][:, hh * 128:(hh + 1) * 128]
                            te.ldweights(m2s)
                            i3 = te.matmul(YDP[tb][:, j * 64:(j + 1) * 64], m2s,
                                           XCH[tb][:, j * 64:(j + 1) * 64], start=True, stop=True)
                        i3.then_inc(S["pe"], 1)
                    bl.bump("pe", 1)
                    y_tgt = C["pe"]
                    if P4_LEVEL == "mm":
                        gbt += 1
                        continue

                    @blk.vector
                    def _(v, z=z, h0=h0, tb=tb, gb=gb, y_tgt=y_tgt):
                        v.wait_ge(S["pe"], y_tgt)
                        scb = scA_tm[:, z * 128 + h0:z * 128 + h0 + 8].unsqueeze(2).broadcast_to([128, 8, 64])
                        ys = Y[:, gb * 512:(gb + 1) * 512]
                        t = TMP[tb][:]
                        v.tensor_tensor(t.rearrange("p (a b) -> p a b", a=8),
                                        YOP[tb][:].rearrange("p (a b) -> p a b", a=8), scb, ALU.mult)
                        v.tensor_tensor(t, t, YDP[tb][:], ALU.add)
                        v.tensor_tensor(ys, t, ys, ALU.add)
                        dzb = decayZbc[:, z * 128 + h0:z * 128 + h0 + 8].unsqueeze(2).broadcast_to([128, 8, 64])
                        cf = CARF[:, gb * 512:(gb + 1) * 512]
                        v.tensor_tensor(cf.rearrange("p (a b) -> p a b", a=8),
                                        cf.rearrange("p (a b) -> p a b", a=8), dzb, ALU.mult)
                        v.tensor_tensor(cf, cf, STP[tb][:], ALU.add)
                        if "stpgb" in dbg and z == 5 and gb == 9:
                            v.tensor_copy(DBGT[:, 0:512], STP[tb][:])
                            v.tensor_copy(DBGT[:, 512:1024], XCD[tb][:])
                            v.tensor_copy(DBGT[:, 1024:1536], XCH[tb][:])
                            v.tensor_copy(DBGT[:, 1536:1664], BTM[:, (gb // 2) * 128:(gb // 2 + 1) * 128])
                        v.tensor_copy(CARB[:, gb * 512:(gb + 1) * 512], cf).then_inc(S["dv"], 1)
                    bl.bump("dv", 1)
                    psY_hist[tb] = C["dv"]
                    gbt += 1

            dv_y = C["dv"]

            @blk.gpsimd
            def _(g, z=z, dv_y=dv_y):
                g.wait_ge(S["dv"], dv_y)
                g.dma_start(out=io["Y_sp"][z], in_=Y[:]).then_inc(S["dg"], 16)
                if "carrytrace" in dbg:
                    g.dma_start(out=io["dbg_carry"][z], in_=CARF[:]).then_inc(S["dg"], 16)
            bl.bump("dg", 32 if "carrytrace" in dbg else 16)

        dv_fin = C["dv"]

        @blk.gpsimd
        def _(g):
            g.wait_ge(S["dv"], dv_fin)
            g.dma_start(out=io["carry_out"], in_=CARF[:]).then_inc(S["dg"], 16)
            if "stpgb" in dbg:
                g.dma_start(out=io["dbg_stp"], in_=DBGT[:]).then_inc(S["dg"], 16)
            if "stp" in dbg:
                g.dma_start(out=io["dbg_dtdec"], in_=dtdec_tm).then_inc(S["dg"], 16)
                g.dma_start(out=io["dbg_dtcol"], in_=dtcol_tm).then_inc(S["dg"], 16)
        bl.bump("dg", 16 + (16 if "stpgb" in dbg else 0) + (32 if "stp" in dbg else 0))

        bl.barrier()
    gst.close()


def build_L1(dbg=()):
    nc = bass.Bass("TRN2", target_bir_lowering=False)
    io = {}
    ei = lambda n, s, d: nc.dram_tensor(n, s, d, kind="ExternalInput").ap()
    eo = lambda n, s, d: nc.dram_tensor(n, s, d, kind="ExternalOutput").ap()
    it = lambda n, s, d: nc.dram_tensor(n, s, d, kind="Internal").ap()
    io["x_comb"] = ei("x_comb", [1040, HID], F32)
    io["W_in"] = ei("W_in", [E_TOT, HID], F32)
    io["conv_wr"] = ei("conv_wr", [128, 320], F32)
    io["conv_br"] = ei("conv_br", [128, 80], F32)
    io["dt_bias_c"] = ei("dt_bias_c", [128, 1], F32)
    io["A_log_c"] = ei("A_log_c", [128, 1], F32)
    io["D_c"] = ei("D_c", [128, 1], F32)
    io["D_row"] = ei("D_row", [1, 128], BF16)
    io["ident64"] = ei("ident64", [128, 64], BF16)
    io["identbf"] = ei("identbf", [128, 128], BF16)
    io["identf"] = ei("identf", [128, 128], F32)
    io["maskadd"] = ei("maskadd", [128, 128], F32)
    io["ones_bf"] = ei("ones_bf", [1, 128], BF16)
    io["ustrict"] = ei("ustrict", [8, 8], F32)
    io["Y_sp"] = eo("Y_sp", [NZ, 128, INTER], BF16)
    io["gate_sp"] = eo("gate_sp", [64, 128, T], BF16)
    io["C_sp"] = eo("C_sp", [8, 128, T], BF16)
    io["carry_out"] = eo("carry_out", [128, INTER], F32)
    io["expac"] = eo("expac", [128, T], F32)
    io["gamzh"] = eo("gamzh", [8, 128], F32)
    io["xbf"] = it("xbf", [1040, HID], BF16)
    io["wbf"] = [it(f"wbf{i}", [256, HID], BF16) for i in range(2)]
    io["hs_sp"] = it("hs_sp", [64, 128, T], BF16)
    io["acum_d"] = it("acum_d", [NZ, 128, 128], F32)
    io["az_d"] = it("az_d", [8, 128], F32)
    if "p3" in dbg:
        for nm in ["dbg_dtraw", "dbg_dtv", "dbg_acum", "dbg_acumtm", "dbg_adt"]:
            io[nm] = eo(nm, [128, T], F32)
        io["dbg_cols"] = eo("dbg_cols", [128, 8], F32)
        io["dbg_small"] = eo("dbg_small", [128, 64], F32)
    if "carrytrace" in dbg:
        io["dbg_carry"] = eo("dbg_carry", [NZ, 128, INTER], F32)
    if "stp" in dbg:
        io["dbg_stp"] = eo("dbg_stp", [128, 2048], F32)
        io["dbg_dtdec"] = eo("dbg_dtdec", [128, T], F32)
        io["dbg_dtdec_pre"] = eo("dbg_dtdec_pre", [128, T], F32)
        io["dbg_scr"] = eo("dbg_scr", [128, T], F32)
        io["dbg_tmp2"] = eo("dbg_tmp2", [128, T], F32)
        io["dbg_dtvfm"] = eo("dbg_dtvfm", [128, T], F32)
        io["dbg_snap"] = eo("dbg_snap", [128, T], F32)
        io["dbg_dtcol"] = eo("dbg_dtcol", [128, T], F32)
    with contextlib.ExitStack() as stack:
        block = stack.enter_context(nc.Block())
        bl = Bld(nc, block, stack)
        emit_L1(nc, bl, io, dbg)
    return nc


def _host_consts():
    i64 = np.vstack([np.eye(64), np.eye(64)]).astype(np.float32).astype(BF)
    ibf = np.eye(128, dtype=np.float32).astype(BF)
    idf = np.eye(128, dtype=np.float32)
    sidx = np.arange(128)[:, None]
    lidx = np.arange(128)[None, :]
    maskadd = np.where(lidx >= sidx, 0.0, -1e30).astype(np.float32)
    ones_bf = np.ones((1, 128), dtype=np.float32).astype(BF)
    return i64, ibf, idf, maskadd, ones_bf


def l1_inputs(inputs, c):
    b, half = c // 2, c % 2
    x = np.asarray(inputs["L0_attn_x"])
    t0 = half * T
    xc = np.zeros((1040, HID), dtype=np.float32)
    xc[16:] = x[b, t0:t0 + T]
    if half == 1:
        xc[13:16] = x[b, t0 - 3:t0]
    i64, ibf, idf, maskadd, ones_bf = _host_consts()
    conv_w = np.asarray(inputs["L0_attn_conv_w"]).reshape(CONVD, 4)
    conv_wr = np.ascontiguousarray(conv_w.reshape(80, 128, 4).transpose(1, 0, 2).reshape(128, 320))
    conv_br = np.ascontiguousarray(np.asarray(inputs["L0_attn_conv_b"]).reshape(80, 128).T)
    D = np.asarray(inputs["L0_attn_D"]).astype(np.float32)
    return {
        "x_comb": xc,
        "W_in": np.ascontiguousarray(np.asarray(inputs["L0_attn_in_proj_w"], dtype=np.float32)),
        "conv_wr": conv_wr,
        "conv_br": conv_br,
        "dt_bias_c": np.ascontiguousarray(np.asarray(inputs["L0_attn_dt_bias"]).reshape(128, 1).astype(np.float32)),
        "A_log_c": np.ascontiguousarray(np.asarray(inputs["L0_attn_A_log"]).reshape(128, 1).astype(np.float32)),
        "D_c": np.ascontiguousarray(D.reshape(128, 1)),
        "D_row": np.ascontiguousarray(D.reshape(1, 128)).astype(BF),
        "ident64": i64,
        "identbf": ibf,
        "identf": idf,
        "maskadd": maskadd,
        "ones_bf": ones_bf,
        "ustrict": np.triu(np.ones((8, 8), dtype=np.float32), k=1),
    }





def emit_L2(nc, bl, io, dbg):
    blk = bl.block
    S = bl.sems
    C = bl.cnt
    sb = lambda name, shape, dt: nc.sbuf_tensor(_u(name), shape, dt)
    ps = lambda name, shape, dt: nc.psum_tensor(_u(name), shape, dt)

    gst = contextlib.ExitStack()
    IDBF = gst.enter_context(sb("IDBF", [128, 128], BF16))
    IDF = gst.enter_context(sb("IDF", [128, 128], F32))
    ONESB = gst.enter_context(sb("ONESB", [1, 128], BF16))
    NW = gst.enter_context(sb("NW", [128, 64], F32))
    COLS = gst.enter_context(sb("COLS", [128, 4], F32))
    XGT = gst.enter_context(sb("XGT", [64, 1024], BF16))
    XAB = gst.enter_context(sb("XAB", [16, 1024], BF16))
    LB = gst.enter_context(sb("LB", [16, 4096], BF16))
    BR = gst.enter_context(sb("BR", [1, 4096], BF16))
    SS = gst.enter_context(sb("SS", [128, 64], F32))
    SQ2 = gst.enter_context(sb("SQ2", [128, 64], F32))
    XG = gst.enter_context(sb("XG", [128, 512], F32))

    bl.sem("dg", "ds", "pe", "dv", "ac", "pf", "mm", "xw", "gw", "hs", "gs", "peW")

    @blk.sync
    def _(sy):
        for dst, srcap in [(IDBF[:], io["identbf"]), (IDF[:], io["identf"]),
                           (ONESB[:], io["ones_bf"]), (NW[:], io["norm_w_fm"]),
                           (COLS[:, 0:1], io["flag_col"])]:
            sy.dma_start(out=dst, in_=srcap).then_inc(S["ds"], 16)
    bl.bump("ds", 16 * 5)

    @blk.vector
    def _(v):
        v.memset(COLS[:, 1:2], EPS)
        v.memset(COLS[:, 2:3], 1.0).then_inc(S["dv"], 1)
    bl.bump("dv", 1)

    @blk.gpsimd
    def _(g):
        for i in range(8):
            g.dma_start(out=io["wop_bf"][i * 512:(i + 1) * 512, :],
                        in_=io["W_op"][i * 512:(i + 1) * 512, :]).then_inc(S["gw"], 16)
        g.dma_start(out=io["carry_bf_d"], in_=io["carry_in"]).then_inc(S["gw"], 16)
    bl.bump("gw", 16 * 9)
    wop_done = C["gw"]

    with contextlib.ExitStack() as q1:
        EXPAC = q1.enter_context(sb("EXPAC", [128, 1024], F32))
        GBC = q1.enter_context(sb("GBC", [128, 1024], F32))
        SCB = q1.enter_context(sb("SCB", [128, 1024], F32))
        CARB2 = q1.enter_context(sb("CARB2", [128, 8192], BF16))
        Y2 = q1.enter_context(sb("Y2", [128, 8192], BF16))
        GT = q1.enter_context(sb("GT", [128, 8192], BF16))
        G2 = q1.enter_context(sb("G2", [128, 8192], F32))
        C2 = q1.enter_context(sb("C2", [128, 8, 128], BF16))
        U2 = q1.enter_context(sb("U2", [128, 8192], BF16))
        T1 = q1.enter_context(sb("T1", [128, 1024], F32))
        RST = q1.enter_context(sb("RST", [128, 8], F32))
        VARS = q1.enter_context(sb("VARS", [128, 8], F32))
        DYP = [q1.enter_context(ps("DYP", [128, 1024], F32)) for _ in range(2)]

        @blk.sync
        def _(sy):
            sy.dma_start(out=EXPAC[:], in_=io["expac"]).then_inc(S["ds"], 16)
            src = io["gamzh"].rearrange("a b -> (a b)").unsqueeze(0).partition_broadcast(128)
            sy.dma_start(out=GBC[:], in_=src).then_inc(S["ds"], 16)
        bl.bump("ds", 32)
        q1_loads = C["ds"]

        @blk.gpsimd
        def _(g):
            g.wait_ge(S["gw"], wop_done)
            g.dma_start(out=CARB2[:], in_=io["carry_bf_d"]).then_inc(S["dg"], 16)
        bl.bump("dg", 16)
        carb_done = C["dg"]

        w_dv0 = C["dv"]

        @blk.vector
        def _(v):
            v.wait_ge(S["ds"], q1_loads)
            v.wait_ge(S["dv"], w_dv0)
            v.tensor_tensor(SCB[:], EXPAC[:], GBC[:], ALU.mult)
            v.wait_ge(S["dv"], w_dv0)
            v.tensor_scalar(SCB[:], SCB[:], COLS[:, 0:1], None, ALU.mult).then_inc(S["dv"], 1)
        bl.bump("dv", 1)
        scb_done = C["dv"]

        for z in range(NZ):
            zc = slice(z * CH, (z + 1) * CH)
            dv_pre = C["dv"]

            @blk.sync
            def _(sy, z=z, zc=zc, dv_pre=dv_pre):
                if z >= 1:
                    sy.wait_ge(S["dv"], dv_pre)
                sy.dma_start(out=Y2[:], in_=io["Y_sp"][z]).then_inc(S["ds"], 16)
                sy.dma_start(out=C2[:], in_=io["C_sp"][:, :, zc].transpose([1, 0, 2])).then_inc(S["ds"], 16)
                for e in range(64):
                    sy.dma_start(out=GT[:, e * 128:(e + 1) * 128],
                                 in_=io["gate_sp"][e][0:128, zc], transpose=True).then_inc(S["ds"], 16)
            bl.bump("ds", 16 * 66)
            ld_tgt = C["ds"]

            for g_ in range(G):
                pp = g_ % 2
                w_ev = C["dv"]

                @blk.tensor
                def _(te, z=z, g_=g_, pp=pp, ld_tgt=ld_tgt, w_ev=w_ev, carb_done=carb_done):
                    te.wait_ge(S["ds"], ld_tgt)
                    te.wait_ge(S["dg"], carb_done)
                    if g_ >= 2:
                        te.wait_ge(S["dv"], w_ev)
                    cfm = C2[:, g_, :]
                    te.ldweights(cfm)
                    for j in range(16):
                        h = g_ * 16 + j
                        bank = (j // 8)
                        i2 = te.matmul(DYP[pp][:, bank * 512 + (j % 8) * 64: bank * 512 + (j % 8) * 64 + 64],
                                       cfm, CARB2[:, h * 64:(h + 1) * 64], start=True, stop=True)
                    i2.then_inc(S["pe"], 1)
                bl.bump("pe", 1)
                pe_tgt = C["pe"]

                @blk.vector
                def _(v, z=z, g_=g_, pp=pp, pe_tgt=pe_tgt, scb_done=scb_done):
                    v.wait_ge(S["pe"], pe_tgt)
                    v.wait_ge(S["dv"], scb_done)
                    scb = SCB[:, z * 128 + g_ * 16: z * 128 + (g_ + 1) * 16].unsqueeze(2).broadcast_to([128, 16, 64])
                    v.tensor_tensor(T1[:].rearrange("p (a b) -> p a b", a=16),
                                    DYP[pp][:].rearrange("p (a b) -> p a b", a=16), scb, ALU.mult)
                    ys = Y2[:, g_ * 1024:(g_ + 1) * 1024]
                    v.tensor_tensor(ys, T1[:], ys, ALU.add).then_inc(S["dv"], 1)
                bl.bump("dv", 1)

            dv_corr = C["dv"]

            @blk.vector
            def _(v, z=z, dv_corr=dv_corr):
                v.wait_ge(S["dv"], dv_corr)
                v.tensor_tensor(G2[:], Y2[:], GT[:], ALU.mult).then_inc(S["dv"], 1)
            bl.bump("dv", 1)
            g2_done = C["dv"]

            @blk.scalar
            def _(sc, z=z, g2_done=g2_done):
                sc.wait_ge(S["dv"], g2_done)
                for gg in range(G):
                    i2 = sc.activation(out=T1[:], in_=G2[:, gg * 1024:(gg + 1) * 1024],
                                       func=AF.Square, accum_out=VARS[:, gg:gg + 1])
                i2.then_inc(S["ac"], 1)
                sc.wait_ge(S["ac"], C["ac"] + 1)
                sc.activation(out=RST[:], in_=VARS[:], func=AF.Ln, bias=COLS[:, 1:2], scale=1.0 / 1024.0)
                sc.activation(out=VARS[:], in_=RST[:], func=AF.Exp, scale=-0.5).then_inc(S["ac"], 1)
            bl.bump("ac", 2)
            rst_done = C["ac"]

            @blk.vector
            def _(v, z=z, rst_done=rst_done):
                v.wait_ge(S["ac"], rst_done)
                last = None
                for gg in range(G):
                    last = v.tensor_scalar(U2[:, gg * 1024:(gg + 1) * 1024],
                                           G2[:, gg * 1024:(gg + 1) * 1024],
                                           VARS[:, gg:gg + 1], None, ALU.mult)
                last.then_inc(S["dv"], 1)
            bl.bump("dv", 1)
            u_done = C["dv"]

            @blk.gpsimd
            def _(g, z=z, u_done=u_done):
                g.wait_ge(S["dv"], u_done)
                g.dma_start(out=io["u_bf"][z * 128:(z + 1) * 128, :], in_=U2[:]).then_inc(S["dg"], 16)
            bl.bump("dg", 16)

        bl.barrier()

    with contextlib.ExitStack() as q2:
        UF = q2.enter_context(sb("UF", [128, 64 * 1024], BF16))
        WOT = [q2.enter_context(sb("WOT", [128, 64 * 128], BF16)) for _ in range(2)]
        AS = [q2.enter_context(sb("AS", [128, 1024], F32)) for _ in range(2)]
        STG = [q2.enter_context(sb("STG", [128, 512], F32)) for _ in range(2)]
        OPP = [q2.enter_context(ps("OPP", [128, 1024], F32)) for _ in range(2)]
        TRP = [q2.enter_context(ps("TRP", [128, 512], F32)) for _ in range(2)]

        @blk.sync
        def _(sy):
            for i in range(64):
                sy.dma_start(out=UF[:, i * 1024:(i + 1) * 1024],
                             in_=io["u_bf"][:, i * 128:(i + 1) * 128], transpose=True).then_inc(S["ds"], 16)
        bl.bump("ds", 16 * 64)
        uf_ld = C["ds"]

        @blk.vector
        def _(v):
            v.wait_ge(S["ds"], uf_ld)
            last = None
            for i in range(64):
                last = v.tensor_scalar(UF[:, i * 1024:(i + 1) * 1024], UF[:, i * 1024:(i + 1) * 1024],
                                       NW[:, i:i + 1], None, ALU.mult)
            last.then_inc(S["dv"], 1)
        bl.bump("dv", 1)
        uf_done = C["dv"]

        for ot in range(32):
            pp = ot % 2
            w_pe = C["pe"] - 1 if ot >= 2 else 0

            @blk.sync
            def _(sy, ot=ot, pp=pp, w_pe=w_pe, wd=wop_done):
                sy.wait_ge(S["gw"], wd)
                if w_pe:
                    sy.wait_ge(S["pe"], w_pe)
                for i in range(64):
                    sy.dma_start(out=WOT[pp][:, i * 128:(i + 1) * 128],
                                 in_=io["wop_bf"][ot * 128:(ot + 1) * 128, i * 128:(i + 1) * 128],
                                 transpose=True).then_inc(S["xw"], 16)
            bl.bump("xw", 16 * 64)
            xw_tgt = C["xw"]
            w_ev = C["dv"] - 1 if ot >= 2 else 0

            @blk.tensor
            def _(te, ot=ot, pp=pp, xw_tgt=xw_tgt, uf_done=uf_done, w_ev=w_ev):
                te.wait_ge(S["xw"], xw_tgt)
                te.wait_ge(S["dv"], uf_done)
                if w_ev:
                    te.wait_ge(S["dv"], w_ev)
                for i in range(64):
                    lhsT = WOT[pp][:, i * 128:(i + 1) * 128]
                    te.ldweights(lhsT)
                    te.matmul(OPP[pp][:, 0:512], lhsT, UF[:, i * 1024:i * 1024 + 512],
                              start=(i == 0), stop=(i == 63))
                    i2 = te.matmul(OPP[pp][:, 512:1024], lhsT, UF[:, i * 1024 + 512:(i + 1) * 1024],
                                   start=(i == 0), stop=(i == 63))
                i2.then_inc(S["pe"], 1)
            bl.bump("pe", 1)
            mm_tgt = C["pe"]

            @blk.vector
            def _(v, ot=ot, pp=pp, mm_tgt=mm_tgt):
                v.wait_ge(S["pe"], mm_tgt)
                v.tensor_copy(AS[pp][:], OPP[pp][:]).then_inc(S["dv"], 1)
            bl.bump("dv", 1)
            as_done = C["dv"]

            for hf in range(2):
                w_stg = C["dv"] - 1 if (ot * 2 + hf) >= 2 else 0

                @blk.tensor
                def _(te, ot=ot, pp=pp, hf=hf, as_done=as_done, w_stg=w_stg):
                    te.wait_ge(S["dv"], as_done)
                    if w_stg:
                        te.wait_ge(S["dv"], w_stg)
                    for j in range(4):
                        tt = hf * 4 + j
                        i2 = te.transpose(TRP[hf][:, j * 128:(j + 1) * 128],
                                          AS[pp][:, tt * 128:(tt + 1) * 128], IDF[:])
                    i2.then_inc(S["pe"], 1)
                bl.bump("pe", 1)
                tr_tgt = C["pe"]

                @blk.vector
                def _(v, hf=hf, tr_tgt=tr_tgt):
                    v.wait_ge(S["pe"], tr_tgt)
                    v.tensor_copy(STG[hf][:], TRP[hf][:]).then_inc(S["dv"], 1)
                bl.bump("dv", 1)
                stg_done = C["dv"]

                @blk.gpsimd
                def _(g, ot=ot, hf=hf, stg_done=stg_done):
                    g.wait_ge(S["dv"], stg_done)
                    for j in range(4):
                        tt = hf * 4 + j
                        g.dma_start(out=io["preh1"][tt * 128:(tt + 1) * 128, ot * 128:(ot + 1) * 128],
                                    in_=STG[hf][:, j * 128:(j + 1) * 128]).then_inc(S["dg"], 16)
                bl.bump("dg", 64)

        bl.barrier()

    with contextlib.ExitStack() as q3:
        XS = q3.enter_context(sb("XS", [128, 4096], F32))
        PS1 = q3.enter_context(sb("PS1", [128, 4096], F32))
        PRE = q3.enter_context(sb("PRE", [128, 4096], F32))
        H1 = q3.enter_context(sb("H1", [128, 4096], F32))
        STT1 = q3.enter_context(sb("STT1", [128, 8], F32))

        for tt in range(8):
            dv_pre = C["dv"]

            @blk.sync
            def _(sy, tt=tt, dv_pre=dv_pre):
                if tt >= 1:
                    sy.wait_ge(S["dv"], dv_pre)
                sy.dma_start(out=XS[:], in_=io["x_comb"][16 + tt * 128:16 + (tt + 1) * 128, :]).then_inc(S["ds"], 16)
                sy.dma_start(out=PS1[:], in_=io["preh1"][tt * 128:(tt + 1) * 128, :]).then_inc(S["ds"], 16)
            bl.bump("ds", 32)
            ld_tgt = C["ds"]

            @blk.vector
            def _(v, tt=tt, ld_tgt=ld_tgt):
                v.wait_ge(S["ds"], ld_tgt)
                v.tensor_tensor(PRE[:], XS[:], PS1[:], ALU.add)
                v.tensor_reduce(STT1[:, 0:1].unsqueeze(2), PRE[:].unsqueeze(1), mybir.AxisListType.X, ALU.add).then_inc(S["dv"], 1)
            bl.bump("dv", 1)
            pre_done = C["dv"]

            @blk.scalar
            def _(sc, tt=tt, pre_done=pre_done):
                sc.wait_ge(S["dv"], pre_done)
                sc.activation(out=H1[:], in_=PRE[:], func=AF.Square,
                              accum_out=STT1[:, 1:2]).then_inc(S["ac"], 1)
            bl.bump("ac", 1)
            sq_done = C["ac"]

            @blk.vector
            def _(v, tt=tt, sq_done=sq_done):
                v.wait_ge(S["ac"], sq_done)
                v.tensor_scalar_mul(STT1[:, 2:3], STT1[:, 0:1], 1.0 / 4096.0)
                v.tensor_scalar_mul(STT1[:, 3:4], STT1[:, 1:2], 1.0 / 4096.0).then_inc(S["dv"], 1)
                v.wait_ge(S["dv"], C["dv"] + 1)
                v.tensor_tensor(STT1[:, 4:5], STT1[:, 2:3], STT1[:, 2:3], ALU.mult).then_inc(S["dv"], 1)
                v.wait_ge(S["dv"], C["dv"] + 2)
                v.tensor_tensor(STT1[:, 4:5], STT1[:, 3:4], STT1[:, 4:5], ALU.subtract).then_inc(S["dv"], 1)
            bl.bump("dv", 3)
            var_done = C["dv"]

            @blk.scalar
            def _(sc, tt=tt, var_done=var_done):
                sc.wait_ge(S["dv"], var_done)
                sc.activation(out=STT1[:, 5:6], in_=STT1[:, 4:5], func=AF.Ln, bias=COLS[:, 1:2])
                sc.activation(out=STT1[:, 6:7], in_=STT1[:, 5:6], func=AF.Exp, scale=-0.5).then_inc(S["ac"], 1)
            bl.bump("ac", 1)
            rstd_done = C["ac"]

            @blk.vector
            def _(v, tt=tt, rstd_done=rstd_done):
                v.wait_ge(S["ac"], rstd_done)
                v.tensor_scalar(H1[:], PRE[:], STT1[:, 2:3], STT1[:, 6:7], ALU.subtract, ALU.mult)
                v.tensor_reduce(XG[:, tt * 64:(tt + 1) * 64].unsqueeze(2),
                                H1[:].rearrange("p (g f) -> p g f", g=64),
                                mybir.AxisListType.X, ALU.add).then_inc(S["dv"], 1)
            bl.bump("dv", 1)
            h1_done = C["dv"]

            @blk.gpsimd
            def _(g, tt=tt, h1_done=h1_done):
                g.wait_ge(S["dv"], h1_done)
                g.dma_start(out=io["h1f"][tt * 128:(tt + 1) * 128, :], in_=H1[:]).then_inc(S["dg"], 16)
                g.dma_start(out=io["h1b"][tt * 128:(tt + 1) * 128, :], in_=H1[:]).then_inc(S["dg"], 16)
        bl.bump("dg", 0)
        for _ in range(8):
            bl.bump("dg", 32)

        bl.barrier()

    with contextlib.ExitStack() as q4:
        XT2 = q4.enter_context(sb("XT2", [128, 32 * 1024], BF16))
        WF = [q4.enter_context(sb("WF", [128, 32 * 512], BF16)) for _ in range(2)]
        WQ = q4.enter_context(sb("WQ", [128, 4096], BF16))
        W1 = q4.enter_context(sb("W1", [128, 4096], BF16))
        SC2 = q4.enter_context(sb("SC2", [128, 64], F32))
        ZR = q4.enter_context(sb("ZR", [128, 64], F32))
        ZS = q4.enter_context(sb("ZS", [128, 64], F32))
        ZST = q4.enter_context(sb("ZST", [64, 512], BF16))
        LA = q4.enter_context(sb("LA", [128, 512], BF16))
        H1S = [q4.enter_context(sb("H1S", [128, 512], F32)) for _ in range(2)]
        STG2 = [q4.enter_context(sb("STG2", [128, 512], F32)) for _ in range(2)]
        T2Q4 = [q4.enter_context(sb("T2Q4", [128, 512], F32)) for _ in range(2)]
        FFP = [q4.enter_context(ps("FFP", [128, 512], F32)) for _ in range(8)]

        @blk.sync
        def _(sy):
            for i in range(32):
                sy.dma_start(out=XT2[:, i * 1024:(i + 1) * 1024],
                             in_=io["h1b"][:, i * 128:(i + 1) * 128], transpose=True).then_inc(S["ds"], 16)
        bl.bump("ds", 16 * 32)
        xt2_ld = C["ds"]

        @blk.gpsimd
        def _(g):
            la_src = io["lora_A"].rearrange("(c p) r -> p c r", p=128)
            g.dma_start(out=LA[:].rearrange("p (c r) -> p c r", c=32), in_=la_src).then_inc(S["dg"], 16)
            g.dma_start(out=LB[:], in_=io["lora_B"]).then_inc(S["dg"], 16)
            g.dma_start(out=BR[:], in_=io["bias_f"].unsqueeze(0)).then_inc(S["dg"], 16)
        bl.bump("dg", 48)
        lora_ld = C["dg"]

        w_pe0 = C["pe"]

        @blk.tensor
        def _(te):
            te.wait_ge(S["pe"], w_pe0)
            for j in range(8):
                i2 = te.transpose(FFP[j][0:64, 0:128], XG[:, j * 64:(j + 1) * 64], IDF[:])
            i2.then_inc(S["pe"], 1)
        bl.bump("pe", 1)
        xgt_tr = C["pe"]

        @blk.vector
        def _(v):
            v.wait_ge(S["pe"], xgt_tr)
            last = None
            for j in range(8):
                last = v.tensor_copy(XGT[0:64, j * 128:(j + 1) * 128], FFP[j][0:64, 0:128])
            last.then_inc(S["dv"], 1)
        bl.bump("dv", 1)
        xgt_done = C["dv"]

        @blk.tensor
        def _(te):
            te.wait_ge(S["ds"], xt2_ld)
            te.wait_ge(S["dg"], lora_ld)
            te.wait_ge(S["dv"], xgt_done)
            for i in range(32):
                lhsT = LA[:, i * 16:(i + 1) * 16]
                te.ldweights(lhsT)
                te.matmul(FFP[0][0:16, 0:512], lhsT, XT2[:, i * 1024:i * 1024 + 512],
                          start=(i == 0), stop=(i == 31))
                i2 = te.matmul(FFP[1][0:16, 0:512], lhsT, XT2[:, i * 1024 + 512:(i + 1) * 1024],
                               start=(i == 0), stop=(i == 31))
            i2.then_inc(S["pe"], 1)
        bl.bump("pe", 1)
        xa_done = C["pe"]

        @blk.vector
        def _(v):
            v.wait_ge(S["pe"], xa_done)
            v.tensor_scalar_mul(XAB[:, 0:512], FFP[0][0:16, 0:512], 2.0)
            v.tensor_scalar_mul(XAB[:, 512:1024], FFP[1][0:16, 0:512], 2.0).then_inc(S["dv"], 1)
        bl.bump("dv", 1)
        xab_done = C["dv"]

        wf_xw_hist = [0, 0]
        for ob in range(8):
            pp = ob % 2
            w_w1d = wf_xw_hist[pp]

            for otl in range(4):
                o0 = ob * 512 + otl * 128
                dq_pre = C["dv"]

                @blk.gpsimd
                def _(g, o0=o0, dq_pre=dq_pre):
                    if dq_pre:
                        g.wait_ge(S["dv"], dq_pre)
                    g.dma_start(out=WQ[:], in_=io["W_q"][o0:o0 + 128, :]).then_inc(S["dg"], 16)
                bl.bump("dg", 16)
                wq_ld = C["dg"]

                @blk.sync
                def _(sy, o0=o0, dq_pre=dq_pre):
                    if dq_pre:
                        sy.wait_ge(S["dv"], dq_pre)
                    sy.dma_start(out=SC2[:], in_=io["scale_f"][o0:o0 + 128, :]).then_inc(S["ds"], 16)
                    sy.dma_start(out=ZR[:], in_=io["zero_f"][o0:o0 + 128, :]).then_inc(S["ds"], 16)
                bl.bump("ds", 32)
                sz_ld = C["ds"]

                w_spill = C["gs"]

                @blk.vector
                def _(v, o0=o0, wq_ld=wq_ld, sz_ld=sz_ld, w_spill=w_spill):
                    v.wait_ge(S["dg"], wq_ld)
                    v.wait_ge(S["ds"], sz_ld)
                    if w_spill:
                        v.wait_ge(S["gs"], w_spill)
                    scb = SC2[:].unsqueeze(2).broadcast_to([128, 64, 64])
                    v.tensor_tensor(W1[:].rearrange("p (a b) -> p a b", a=64),
                                    WQ[:].rearrange("p (a b) -> p a b", a=64), scb, ALU.mult)
                    v.tensor_tensor(ZS[:], ZR[:], SC2[:], ALU.mult).then_inc(S["dv"], 1)
                bl.bump("dv", 1)
                w1_done = C["dv"]

                @blk.gpsimd
                def _(g, pp=pp, otl=otl, w1_done=w1_done, w_w1d=w_w1d):
                    g.wait_ge(S["dv"], w1_done)
                    if w_w1d:
                        g.wait_ge(S["xw"], w_w1d)
                    g.dma_start(out=io["w1d"][pp][otl * 128:(otl + 1) * 128, :], in_=W1[:]).then_inc(S["gs"], 16)
                bl.bump("gs", 16)

                zt_pre = C["dv"]

                @blk.tensor
                def _(te, otl=otl, w1_done=w1_done, zt_pre=zt_pre):
                    te.wait_ge(S["dv"], w1_done)
                    te.wait_ge(S["dv"], zt_pre)
                    te.transpose(FFP[7][0:64, 0:128], ZS[:, 0:64], IDF[:]).then_inc(S["pe"], 1)
                bl.bump("pe", 1)
                zt_done = C["pe"]

                @blk.vector
                def _(v, otl=otl, zt_done=zt_done):
                    v.wait_ge(S["pe"], zt_done)
                    v.tensor_scalar_mul(ZST[0:64, otl * 128:(otl + 1) * 128], FFP[7][0:64, 0:128], -1.0).then_inc(S["dv"], 1)
                bl.bump("dv", 1)

            dq_done = C["gs"]
            zst_done = C["dv"]

            @blk.sync
            def _(sy, pp=pp, dq_done=dq_done):
                sy.wait_ge(S["gs"], dq_done)
                for i in range(32):
                    sy.dma_start(out=WF[pp][:, i * 512:(i + 1) * 512],
                                 in_=io["w1d"][pp][0:512, i * 128:(i + 1) * 128], transpose=True).then_inc(S["xw"], 16)
            bl.bump("xw", 16 * 32)
            wf_ld = C["xw"]
            wf_xw_hist[pp] = C["xw"]

            h1r_pre = C["dv"]

            w_evac = C["dv"]

            @blk.tensor
            def _(te, ob=ob, pp=pp, wf_ld=wf_ld, xab_done=xab_done, zst_done=zst_done, w_evac=w_evac):
                te.wait_ge(S["xw"], wf_ld)
                te.wait_ge(S["dv"], max(xab_done, zst_done, w_evac))
                for tt in range(8):
                    te.ldweights(XGT[:, tt * 128:(tt + 1) * 128])
                    te.matmul(FFP[tt][:, 0:512], XGT[:, tt * 128:(tt + 1) * 128],
                              ZST[0:64, :], start=True, stop=False)
                    te.matmul(FFP[tt][:, 0:512], ONESB[0:1, :],
                              BR[0:1, ob * 512:(ob + 1) * 512], start=False, stop=False)
                    te.matmul(FFP[tt][:, 0:512], XAB[:, tt * 128:(tt + 1) * 128],
                              LB[:, ob * 512:(ob + 1) * 512], start=False, stop=False)
                for i in range(32):
                    for tt in range(8):
                        te.ldweights(XT2[:, i * 1024 + tt * 128: i * 1024 + (tt + 1) * 128])
                        i2 = te.matmul(FFP[tt][:, 0:512], XT2[:, i * 1024 + tt * 128: i * 1024 + (tt + 1) * 128],
                                       WF[pp][:, i * 512:(i + 1) * 512], start=False, stop=(i == 31))
                i2.then_inc(S["pe"], 1)
            bl.bump("pe", 1)
            ff_done = C["pe"]

            for tt in range(8):
                sp = tt % 2
                ld_pre = C["dv"]

                @blk.sync
                def _(sy, ob=ob, tt=tt, sp=sp, ld_pre=ld_pre):
                    if ld_pre:
                        sy.wait_ge(S["dv"], ld_pre)
                    sy.dma_start(out=H1S[sp][:],
                                 in_=io["h1f"][tt * 128:(tt + 1) * 128, ob * 512:(ob + 1) * 512]).then_inc(S["ds"], 16)
                bl.bump("ds", 16)
                h1s_ld = C["ds"]

                @blk.vector
                def _(v, ob=ob, tt=tt, sp=sp, h1s_ld=h1s_ld, ff_done=ff_done):
                    v.wait_ge(S["pe"], ff_done)
                    v.wait_ge(S["ds"], h1s_ld)
                    v.tensor_tensor(STG2[sp][:], FFP[tt][:, 0:512], H1S[sp][:], ALU.add)
                    v.tensor_reduce(SS[:, tt * 8 + ob:tt * 8 + ob + 1].unsqueeze(2), STG2[sp][:].unsqueeze(1),
                                    mybir.AxisListType.X, ALU.add).then_inc(S["dv"], 1)
                bl.bump("dv", 1)
                st_done = C["dv"]

                @blk.scalar
                def _(sc, ob=ob, tt=tt, sp=sp, st_done=st_done):
                    sc.wait_ge(S["dv"], st_done)
                    sc.activation(out=T2Q4[sp][:], in_=STG2[sp][:], func=AF.Square,
                                  accum_out=SQ2[:, tt * 8 + ob:tt * 8 + ob + 1]).then_inc(S["ac"], 1)
                bl.bump("ac", 1)
                sq_done2 = C["ac"]

                @blk.gpsimd
                def _(g, ob=ob, tt=tt, sp=sp, sq_done2=sq_done2):
                    g.wait_ge(S["ac"], sq_done2)
                    g.dma_start(out=io["h2pre"][tt * 128:(tt + 1) * 128, ob * 512:(ob + 1) * 512],
                                in_=STG2[sp][:]).then_inc(S["dg"], 16)
                bl.bump("dg", 16)

        bl.barrier()

    with contextlib.ExitStack() as q5:
        HP = q5.enter_context(sb("HP", [128, 4096], F32))
        OUTS = q5.enter_context(sb("OUTS", [128, 4096], F32))
        STT2 = q5.enter_context(sb("STT2", [128, 8], F32))

        for tt in range(8):
            dv_pre = C["dv"]

            @blk.sync
            def _(sy, tt=tt, dv_pre=dv_pre):
                if tt >= 1:
                    sy.wait_ge(S["dv"], dv_pre)
                sy.dma_start(out=HP[:], in_=io["h2pre"][tt * 128:(tt + 1) * 128, :]).then_inc(S["ds"], 16)
            bl.bump("ds", 16)
            ld_tgt = C["ds"]

            @blk.vector
            def _(v, tt=tt, ld_tgt=ld_tgt):
                v.wait_ge(S["ds"], ld_tgt)
                v.tensor_reduce(STT2[:, 0:1].unsqueeze(2), SS[:, tt * 8:(tt + 1) * 8].unsqueeze(1),
                                mybir.AxisListType.X, ALU.add)
                v.tensor_reduce(STT2[:, 1:2].unsqueeze(2), SQ2[:, tt * 8:(tt + 1) * 8].unsqueeze(1),
                                mybir.AxisListType.X, ALU.add).then_inc(S["dv"], 1)
                v.wait_ge(S["dv"], C["dv"] + 1)
                v.tensor_scalar_mul(STT2[:, 2:3], STT2[:, 0:1], 1.0 / 4096.0)
                v.tensor_scalar_mul(STT2[:, 3:4], STT2[:, 1:2], 1.0 / 4096.0).then_inc(S["dv"], 1)
                v.wait_ge(S["dv"], C["dv"] + 2)
                v.tensor_tensor(STT2[:, 4:5], STT2[:, 2:3], STT2[:, 2:3], ALU.mult).then_inc(S["dv"], 1)
                v.wait_ge(S["dv"], C["dv"] + 3)
                v.tensor_tensor(STT2[:, 4:5], STT2[:, 3:4], STT2[:, 4:5], ALU.subtract).then_inc(S["dv"], 1)
            bl.bump("dv", 4)
            var_done = C["dv"]

            @blk.scalar
            def _(sc, tt=tt, var_done=var_done):
                sc.wait_ge(S["dv"], var_done)
                sc.activation(out=STT2[:, 5:6], in_=STT2[:, 4:5], func=AF.Ln, bias=COLS[:, 1:2])
                sc.activation(out=STT2[:, 6:7], in_=STT2[:, 5:6], func=AF.Exp, scale=-0.5).then_inc(S["ac"], 1)
            bl.bump("ac", 1)
            rstd_done = C["ac"]

            @blk.vector
            def _(v, tt=tt, rstd_done=rstd_done):
                v.wait_ge(S["ac"], rstd_done)
                v.tensor_scalar(OUTS[:], HP[:], STT2[:, 2:3], STT2[:, 6:7],
                                ALU.subtract, ALU.mult).then_inc(S["dv"], 1)
            bl.bump("dv", 1)
            o_done = C["dv"]

            @blk.gpsimd
            def _(g, tt=tt, o_done=o_done):
                g.wait_ge(S["dv"], o_done)
                g.dma_start(out=io["out"][tt * 128:(tt + 1) * 128, :], in_=OUTS[:]).then_inc(S["dg"], 16)
            bl.bump("dg", 16)

        bl.barrier()
    gst.close()


def build_L2(dbg=()):
    nc = bass.Bass("TRN2", target_bir_lowering=False)
    io = {}
    ei = lambda n, s, d: nc.dram_tensor(n, s, d, kind="ExternalInput").ap()
    eo = lambda n, s, d: nc.dram_tensor(n, s, d, kind="ExternalOutput").ap()
    it = lambda n, s, d: nc.dram_tensor(n, s, d, kind="Internal").ap()
    io["x_comb"] = ei("x_comb", [1040, HID], F32)
    io["Y_sp"] = ei("Y_sp", [NZ, 128, INTER], BF16)
    io["gate_sp"] = ei("gate_sp", [64, 128, T], BF16)
    io["C_sp"] = ei("C_sp", [8, 128, T], BF16)
    io["carry_in"] = ei("carry_in", [128, INTER], F32)
    io["expac"] = ei("expac", [128, T], F32)
    io["gamzh"] = ei("gamzh", [8, 128], F32)
    io["flag_col"] = ei("flag_col", [128, 1], F32)
    io["W_op"] = ei("W_op", [HID, INTER], F32)
    io["W_q"] = ei("W_q", [HID, HID], I32)
    io["scale_f"] = ei("scale_f", [HID, 64], F32)
    io["zero_f"] = ei("zero_f", [HID, 64], F32)
    io["lora_A"] = ei("lora_A", [HID, 16], F32)
    io["lora_B"] = ei("lora_B", [16, HID], F32)
    io["bias_f"] = ei("bias_f", [HID], F32)
    io["norm_w_fm"] = ei("norm_w_fm", [128, 64], F32)
    io["identbf"] = ei("identbf", [128, 128], BF16)
    io["identf"] = ei("identf", [128, 128], F32)
    io["ones_bf"] = ei("ones_bf", [1, 128], BF16)
    io["out"] = eo("out", [T, HID], F32)
    io["u_bf"] = it("u_bf", [T, INTER], BF16)
    io["wop_bf"] = it("wop_bf", [HID, INTER], BF16)
    io["carry_bf_d"] = it("carry_bf_d", [128, INTER], BF16)
    dio = eo if "l2" in dbg else it
    io["preh1"] = dio("preh1", [T, HID], F32)
    io["h1f"] = dio("h1f", [T, HID], F32)
    io["h1b"] = dio("h1b", [T, HID], BF16)
    io["w1d"] = [it(f"w1d{i}", [512, HID], BF16) for i in range(2)]
    io["h2pre"] = dio("h2pre", [T, HID], F32)
    io["u_bf"] = dio("u_bf2", [T, INTER], BF16) if "l2" in dbg else io["u_bf"]
    with contextlib.ExitStack() as stack:
        block = stack.enter_context(nc.Block())
        bl = Bld(nc, block, stack)
        emit_L2(nc, bl, io, dbg)
    return nc


def l2_inputs(inputs, c, carry_in):
    b, half = c // 2, c % 2
    x = np.asarray(inputs["L0_attn_x"])
    t0 = half * T
    xc = np.zeros((1040, HID), dtype=np.float32)
    xc[16:] = x[b, t0:t0 + T]
    if half == 1:
        xc[13:16] = x[b, t0 - 3:t0]
    i64, ibf, idf, maskadd, ones_bf = _host_consts()
    nw = np.ascontiguousarray(np.asarray(inputs["L0_attn_norm_w"]).reshape(64, 128).T).astype(np.float32)
    return {
        "x_comb": xc,
        "carry_in": np.ascontiguousarray(carry_in.astype(np.float32)),
        "flag_col": np.full((128, 1), 1.0 if half == 1 else 0.0, dtype=np.float32),
        "W_op": np.ascontiguousarray(np.asarray(inputs["L0_attn_out_proj_w"], dtype=np.float32)),
        "W_q": np.ascontiguousarray(np.asarray(inputs["L0_ffn_W_q"], dtype=np.int32)),
        "scale_f": np.ascontiguousarray(np.asarray(inputs["L0_ffn_scale"], dtype=np.float32)),
        "zero_f": np.ascontiguousarray(np.asarray(inputs["L0_ffn_zero"], dtype=np.float32)),
        "lora_A": np.ascontiguousarray(np.asarray(inputs["L0_ffn_lora_A"], dtype=np.float32)),
        "lora_B": np.ascontiguousarray(np.asarray(inputs["L0_ffn_lora_B"], dtype=np.float32)),
        "bias_f": np.ascontiguousarray(np.asarray(inputs["L0_ffn_bias"], dtype=np.float32)),
        "norm_w_fm": nw,
        "identbf": ibf,
        "identf": idf,
        "ones_bf": ones_bf,
    }


_NC_CACHE = {}


def kernel(**inputs):
    if "L1" not in _NC_CACHE:
        _NC_CACHE["L1"] = build_L1()
        _NC_CACHE["L2"] = build_L2()
    nc1 = _NC_CACHE["L1"]
    nc2 = _NC_CACHE["L2"]
    in1 = [l1_inputs(inputs, c) for c in range(8)]
    r1 = run_bass_kernel_spmd(nc1, in1, core_ids=list(range(8)))
    in2 = []
    for c in range(8):
        if c % 2 == 1:
            carry = np.asarray(r1.results[c - 1]["carry_out"])
        else:
            carry = np.zeros((128, INTER), dtype=np.float32)
        m = l2_inputs(inputs, c, carry)
        res1 = r1.results[c]
        m["Y_sp"] = np.asarray(res1["Y_sp"])
        m["gate_sp"] = np.asarray(res1["gate_sp"])
        m["C_sp"] = np.asarray(res1["C_sp"])
        m["expac"] = np.asarray(res1["expac"])
        m["gamzh"] = np.asarray(res1["gamzh"])
        in2.append(m)
    r2 = run_bass_kernel_spmd(nc2, in2, core_ids=list(range(8)))
    out = np.zeros((4, 2048, HID), dtype=np.float32)
    for c in range(8):
        b, half = c // 2, c % 2
        out[b, half * T:(half + 1) * T] = np.asarray(r2.results[c]["out"])
    return out

